# version 07 (final)
# speedup vs baseline: 1.9816x; 1.0425x over previous
import numpy as np
import concourse.bass as bass
import concourse.mybir as mybir
from concourse.bass_utils import run_bass_kernel_spmd

F32 = mybir.dt.float32
BF16 = mybir.dt.bfloat16
AF = mybir.ActivationFunctionType
ALU = mybir.AluOpType
AX = mybir.AxisListType

S = 2048
D = 1024
NT = S // 128
DC = D // 128
H1 = 8
H2 = 16
E = 8
F = 2048
FC = F // 128
SC4 = 4
GW = 2432
LN_EPS = 1e-5
RMS_EPS = 1e-6



def _tables():
    f64 = np.float64
    pos = np.arange(S, dtype=f64)

    half = 64
    sv = (np.arange(0, 128, 2, dtype=f64) + 0.4 * 128) / (1.4 * 128)
    scale = sv[None, :] ** (pos[:, None] / 512.0)
    invf = 1.0 / 10000.0 ** (np.arange(half, dtype=f64) / half)
    ang = pos[:, None] * invf[None, :]
    sin, cos = np.sin(ang), np.cos(ang)

    def fm_pack(m, signfold):
        t = np.empty((128, S), f64)
        t[0::2, :] = m.T
        t[1::2, :] = m.T
        if signfold:
            t[0::2, :] *= -1.0
        return t.astype(np.float32)

    xq_cos = fm_pack(cos * scale, False)
    xq_sin = fm_pack(sin * scale, True)
    xk_cos = fm_pack(cos / scale, False)
    xk_sin = fm_pack(sin / scale, True)

    gam = 1.0 - np.exp(np.linspace(np.log(1 / 32), np.log(1 / 512), H1))
    u = np.arange(GW, dtype=f64) - 384.0
    tp = np.arange(128, dtype=f64)
    expo = u[None, :] - tp[:, None]
    gmask = np.zeros((H1, 128, GW), np.float32)
    for h in range(H1):
        gmask[h] = np.where(expo >= 0, gam[h] ** np.maximum(expo, 0.0), 0.0
                            ).astype(np.float32)

    invf2 = 1.0 / 10000.0 ** (np.arange(0, 64, 2, dtype=f64) / 64)
    ang2 = pos[:, None] * invf2[None, :]
    c2, s2 = np.cos(ang2), np.sin(ang2)

    def rope_pack(m, signfold):
        t = np.empty((128, S), f64)
        for blk in (0, 64):
            t[blk:blk + 32, :] = m.T
            t[blk + 32:blk + 64, :] = m.T
        if signfold:
            t[0:32] *= -1.0
            t[64:96] *= -1.0
        return t.astype(np.float32)

    rope_cos = rope_pack(c2, False)
    rope_sin = rope_pack(s2, True)

    perm2 = np.zeros((128, 128), np.float32)
    perm64 = np.zeros((128, 128), np.float32)
    for i in range(128):
        perm2[i ^ 1, i] = 1.0
        blk, j = i & 64, i & 63
        perm64[blk + ((j + 32) & 63), i] = 1.0
    ident = np.eye(128, dtype=np.float32)
    esel = np.zeros((8, 1024), np.float32)
    for e in range(8):
        esel[e, e * 128:(e + 1) * 128] = 1.0
    return dict(esel=esel, xq_cos=xq_cos, xq_sin=xq_sin, xk_cos=xk_cos, xk_sin=xk_sin,
                gmask=gmask, rope_cos=rope_cos, rope_sin=rope_sin,
                perm2=perm2, perm64=perm64, ident=ident)


def _prep_shared(inputs):
    c = np.ascontiguousarray
    f32 = np.float32
    d = {}
    d["wq"] = c(inputs["L0_attn_W_Q"], dtype=f32)
    d["wk"] = c(inputs["L0_attn_W_K"], dtype=f32)
    d["wv"] = c(inputs["L0_attn_W_V"], dtype=f32)
    d["wg"] = c(inputs["L0_attn_W_G"], dtype=f32)
    d["wo"] = c(inputs["L0_attn_W_O"], dtype=f32)
    d["gnw_col"] = c(np.asarray(inputs["L0_attn_gn_weight"]).reshape(8, 128).T, dtype=f32)
    d["gnb_col"] = c(np.asarray(inputs["L0_attn_gn_bias"]).reshape(8, 128).T, dtype=f32)
    d["n1w_col"] = c(np.asarray(inputs["L0_ffn_norm1_w"]).reshape(8, 128).T, dtype=f32)
    d["n2w_col"] = c(np.asarray(inputs["L0_ffn_norm2_w"]).reshape(8, 128).T, dtype=f32)
    d["qkvwT"] = c(np.asarray(inputs["L0_ffn_qkv_w"]).T, dtype=f32)
    d["qkvb_col"] = c(np.asarray(inputs["L0_ffn_qkv_b"]).reshape(24, 128).T, dtype=f32)
    d["vb_row"] = c(np.asarray(inputs["L0_ffn_qkv_b"])[2048:].reshape(1, 1024), dtype=f32)
    d["outwT"] = c(np.asarray(inputs["L0_ffn_out_w"]).T, dtype=f32)
    d["outb_col"] = c(np.asarray(inputs["L0_ffn_out_b"]).reshape(8, 128).T, dtype=f32)
    d["gatewT"] = c(np.asarray(inputs["L0_ffn_gate_w"]).T, dtype=f32)
    d["w1T"] = c(np.asarray(inputs["L0_ffn_w1"]).transpose(0, 2, 1), dtype=f32)
    d["w2T"] = c(np.asarray(inputs["L0_ffn_w2"]).transpose(0, 2, 1), dtype=f32)
    d["w3T"] = c(np.asarray(inputs["L0_ffn_w3"]).transpose(0, 2, 1), dtype=f32)
    d["b1c"] = c(np.asarray(inputs["L0_ffn_b1"]).reshape(8, 16, 128).transpose(0, 2, 1), dtype=f32)
    d["b2c"] = c(np.asarray(inputs["L0_ffn_b2"]).reshape(8, 16, 128).transpose(0, 2, 1), dtype=f32)
    d["b3c"] = c(np.asarray(inputs["L0_ffn_b3"]).reshape(8, 8, 128).transpose(0, 2, 1), dtype=f32)
    def hslab(w):
        n = w.shape[0]
        return c(w.reshape(n, 8, 128, 128).transpose(0, 2, 1, 3)
                 .reshape(n, 128, 1024), dtype=f32)
    d["wqS"] = hslab(d["wq"])
    d["wkS"] = hslab(d["wk"])
    d["wvS"] = hslab(d["wv"])
    pidx = np.arange(128) ^ 1
    d["wqPS"] = hslab(d["wq"][:, :, pidx])
    d["wkPS"] = hslab(d["wk"][:, :, pidx])
    d["wgS"] = c(d["wg"].reshape(8, 128, 8, 128).transpose(2, 1, 0, 3)
                 .reshape(8, 128, 1024), dtype=f32)
    d["qkvS"] = c(d["qkvwT"].reshape(8, 128, 24, 128).transpose(2, 1, 0, 3)
                  .reshape(24, 128, 1024), dtype=f32)
    def wslab(wT):
        return c(wT.reshape(8, 8, 128, 16, 128).transpose(0, 3, 2, 1, 4)
                 .reshape(8, 16, 128, 1024), dtype=f32)
    d["w1S"] = wslab(d["w1T"])
    d["w2S"] = wslab(d["w2T"])
    d["gateS"] = c(d["gatewT"].reshape(8, 128, 8).transpose(1, 0, 2)
                   .reshape(128, 64), dtype=f32)
    d["b1A"] = c(np.asarray(inputs["L0_ffn_b1"]).reshape(8, 16, 128)
                 .transpose(2, 0, 1).reshape(128, 128), dtype=f32)
    d["b2A"] = c(np.asarray(inputs["L0_ffn_b2"]).reshape(8, 16, 128)
                 .transpose(2, 0, 1).reshape(128, 128), dtype=f32)
    d["b3A"] = c(np.asarray(inputs["L0_ffn_b3"]).reshape(8, 8, 128)
                 .transpose(2, 0, 1).reshape(128, 64), dtype=f32)
    d.update(_tables())
    return d



class Ch:
    __slots__ = ("sem", "n")

    def __init__(self, nc, name):
        self.sem = nc.alloc_semaphore(name)
        self.n = 0

    def now(self):
        return (self, self.n)


class Prog:
    ENGINES = ("tensor", "vector", "scalar", "gpsimd", "sync")

    def __init__(self, nc):
        self.nc = nc
        self.recs = {e: [] for e in self.ENGINES}

    def step(self, eng_name, *, wait=(), inc=None, inc_k=1):
        waits = tuple((ch.sem, v) for ch, v in wait if v > 0)
        if inc is not None:
            inc.n += inc_k

        def deco(fn):
            def run(eng, _w=waits, _fn=fn, _inc=inc, _k=inc_k):
                for sem, v in _w:
                    eng.wait_ge(sem, v)
                last = _fn(eng)
                if _inc is not None:
                    assert last is not None, "step with inc must return an instruction"
                    last.then_inc(_inc.sem, _k)
            self.recs[eng_name].append(run)
            return None
        return deco

    def build(self):
        with self.nc.Block() as blk:
            @blk.tensor
            def _(t):
                for r in self.recs["tensor"]:
                    r(t)

            @blk.vector
            def _(v):
                for r in self.recs["vector"]:
                    r(v)

            @blk.scalar
            def _(s):
                for r in self.recs["scalar"]:
                    r(s)

            @blk.gpsimd
            def _(g):
                for r in self.recs["gpsimd"]:
                    r(g)

            @blk.sync
            def _(sy):
                for r in self.recs["sync"]:
                    r(sy)



def build_nc(dbg=(), phases=("R", "F", "M"), nheads=H1, rsub="full"):
    nc = bass.Bass("TRN2", target_bir_lowering=False,
                   dynamic_dma_scratch_size=16384,
                   detect_race_conditions=False)
    dbg = set(dbg)

    dram = {}

    def din(name, shape):
        dram[name] = nc.dram_tensor(name, list(shape), F32, kind="ExternalInput").ap()

    din("xT", (D, S))
    din("wq", (H1, D, 128)); din("wk", (H1, D, 128)); din("wv", (H1, D, 128))
    din("wg", (D, D)); din("wo", (D, D))
    din("gnw_col", (128, 8)); din("gnb_col", (128, 8))
    din("n1w_col", (128, 8)); din("n2w_col", (128, 8))
    din("qkvwT", (D, 3 * D)); din("qkvb_col", (128, 24)); din("vb_row", (1, D))
    din("outwT", (D, D)); din("outb_col", (128, 8))
    din("gatewT", (D, E))
    din("w1T", (E, D, F)); din("w2T", (E, D, F)); din("w3T", (E, F, D))
    din("b1c", (E, 128, 16)); din("b2c", (E, 128, 16)); din("b3c", (E, 128, 8))
    din("xq_cos", (128, S)); din("xq_sin", (128, S))
    din("xk_cos", (128, S)); din("xk_sin", (128, S))
    din("gmask", (H1, 128, GW))
    din("rope_cos", (128, S)); din("rope_sin", (128, S))
    din("perm2", (128, 128)); din("perm64", (128, 128)); din("ident", (128, 128))
    din("esel", (8, 1024))
    din("wqS", (H1, 128, 1024)); din("wkS", (H1, 128, 1024)); din("wvS", (H1, 128, 1024))
    din("wqPS", (H1, 128, 1024)); din("wkPS", (H1, 128, 1024))
    din("wgS", (H1, 128, 1024)); din("qkvS", (24, 128, 1024))
    din("w1S", (E, FC, 128, 1024)); din("w2S", (E, FC, 128, 1024))
    din("gateS", (128, 64)); din("b1A", (128, 128)); din("b2A", (128, 128))
    din("b3A", (128, 64))

    out = nc.dram_tensor("out", [D, S], F32, kind="ExternalOutput").ap()
    DBG_SHAPES = {
        "xn1": (128, DC * S), "qx0": (128, S), "kx0": (128, S),
        "v0": (128, NT * 128), "y0": (128, S), "gate0": (128, S),
        "r1": (128, DC * S), "xn2": (128, DC * S), "qr0": (128, S),
        "kr0": (128, S), "vin0": (128, NT * 256), "attnT": (128, DC * S),
        "x1": (128, DC * S), "xr": (128, DC * S), "comb": (128, 2048),
        "gn0": (128, S), "gn1": (128, 2048), "wops": (128, 512), "lgt": (128, 512), "finadd": (128, DC * S),
        "h0": (128, FC * S), "fin": (128, DC * S),
    }
    dbg_out = {}
    for name in dbg:
        dbg_out[name] = nc.dram_tensor("dbg_" + name, list(DBG_SHAPES[name]), F32,
                                       kind="ExternalOutput").ap()

    sb = nc.alloc_sbuf_tensor
    A = sb("A", [128, DC * S], F32)
    XN = sb("XN", [128, DC * S], BF16)
    R3 = sb("R3", [128, DC * S], F32)
    R4 = sb("R4", [128, DC * S], BF16)
    ROWA = sb("ROWA", [128, 2112], F32)
    ROWB = sb("ROWB", [128, 2304], BF16)
    CPERM2 = sb("CPERM2", [128, 128], BF16)
    CPERM64 = sb("CPERM64", [128, 128], BF16)
    CIDENT = sb("CIDENT", [128, 128], BF16)
    CIDF = sb("CIDF", [128, 128], F32)
    ONESC = sb("ONESC", [128, 2], BF16)
    QKVBC = sb("QKVBC", [128, 24], F32)
    OUTBC = sb("OUTBC", [128, 8], F32)
    GNWC = sb("GNWC", [128, 8], F32)
    GNBC = sb("GNBC", [128, 8], F32)
    N1WC = sb("N1WC", [128, 8], F32)
    N2WC = sb("N2WC", [128, 8], F32)
    BAL1 = sb("BAL1", [128, 128], F32)
    BAL2 = sb("BAL2", [128, 128], F32)
    BAL3 = sb("BAL3", [128, 64], F32)
    GTS = sb("GTS", [128, 64], BF16)

    R3b = R3.ap().bitcast(BF16)
    sbuf_left = nc.sbuf_bytes_remaining

    PB0 = nc.alloc_psum_tensor("pb0", [128, 2048], F32)
    PB1 = nc.alloc_psum_tensor("pb1", [128, 2048], F32)
    PS = [PB0[:, i * 512:(i + 1) * 512] for i in range(4)] + \
         [PB1[:, i * 512:(i + 1) * 512] for i in range(4)]

    P = Prog(nc)
    LD = Ch(nc, "ld")
    LD2 = Ch(nc, "ld2")
    LD3 = Ch(nc, "ld3")
    GP = Ch(nc, "gp")
    PE = Ch(nc, "pe")
    DV = Ch(nc, "dv")
    AC = Ch(nc, "ac")
    ST = Ch(nc, "st")

    def gdma(dst, src, wait=()):
        w = [(LD, LD.n)] + list(wait)

        @P.step("gpsimd", wait=w, inc=LD, inc_k=16)
        def _(g):
            return g.dma_start(dst, src)
        return LD.now()

    def sdma(dst, src):
        @P.step("sync", inc=LD2, inc_k=16)
        def _(sy):
            return sy.dma_start(dst, src)
        return LD2.now()

    rS, rQ, rVar, rMu = 0, 512, 1024, 1536
    rMu2, rLn, rRstd, rMur = rS, rQ, rS, rQ
    cRSh, cRSl, cMUh, cMUl = 0, 512, 1024, 1536
    ONES0 = 2048

    a_ready = []
    for c in range(DC):
        a_ready.append(sdma(A[:, c * S:(c + 1) * S],
                            dram["xT"][c * 128:(c + 1) * 128, :]))
    cw = []
    cw.append(gdma(CPERM2[:, :], dram["perm2"]))
    cw.append(gdma(CPERM64[:, :], dram["perm64"]))
    cw.append(gdma(CIDENT[:, :], dram["ident"]))
    cw.append(sdma(CIDF[:, :], dram["ident"]))
    cw.append(gdma(QKVBC[:, :], dram["qkvb_col"]))
    cw.append(gdma(OUTBC[:, :], dram["outb_col"]))
    cw.append(gdma(GNWC[:, :], dram["gnw_col"]))
    cw.append(gdma(GNBC[:, :], dram["gnb_col"]))
    cw.append(gdma(N1WC[:, :], dram["n1w_col"]))
    cw.append(gdma(N2WC[:, :], dram["n2w_col"]))
    cw.append(gdma(ROWA[64:65, 0:1024], dram["vb_row"]))
    cw.append(sdma(BAL1[:, :], dram["b1A"]))
    cw.append(sdma(BAL2[:, :], dram["b2A"]))
    cw.append(sdma(BAL3[:, :], dram["b3A"]))
    cw.append(gdma(GTS[:, :], dram["gateS"]))
    ball_w = cw[-1]

    @P.step("gpsimd", inc=GP, inc_k=1)
    def _(g):
        g.memset(ONESC[:, :], 1.0)
        return g.memset(ROWB[:, 2048:2304], 1.0)
    cw.append(GP.now())

    rows_ready = (DV, DV.n)
    setup_w = cw + [rows_ready]

    def emit_norm(src, dst, src_ready, *, rms, w_col=None, save_stats=None,
                  ring0=2 * DC * S - 16 * 512, ring_t=None, extra_wait=()):
        eps = RMS_EPS if rms else LN_EPS
        RING = ring_t if ring_t is not None else R3b
        pbS, pbQ, pbB1, pbB2 = PS[4], PS[5], PS[6], PS[7]
        last = None
        for sc in range(SC4):
            s0 = sc * 512
            dv_tiles = []
            for c in range(DC):
                bcp = RING[:, ring0 + c * 512: ring0 + c * 512 + 512]
                bsq = RING[:, ring0 + (8 + c) * 512: ring0 + (8 + c) * 512 + 512]
                @P.step("vector", wait=[src_ready, PE.now()] + list(extra_wait), inc=DV)
                def _(v, bcp=bcp, bsq=bsq, c=c, s0=s0):
                    v.tensor_copy(bcp, src[:, c * S + s0: c * S + s0 + 512])
                    return v.tensor_mul(bsq, bcp, bcp)
                dv_tiles.append(DV.now())
            for c in range(DC):
                bcp = RING[:, ring0 + c * 512: ring0 + c * 512 + 512]
                bsq = RING[:, ring0 + (8 + c) * 512: ring0 + (8 + c) * 512 + 512]

                @P.step("tensor", wait=[dv_tiles[c]],
                        inc=(PE if c == DC - 1 else None))
                def _(t, bcp=bcp, bsq=bsq, c=c):
                    t.matmul(pbS[0:1, :], ONESC[:, 0:1], bcp,
                             start=(c == 0), stop=(c == DC - 1))
                    return t.matmul(pbQ[0:1, :], ONESC[:, 0:1], bsq,
                                    start=(c == 0), stop=(c == DC - 1))
            stats_done = PE.now()

            @P.step("vector", wait=[stats_done, AC.now(), PE.now()], inc=DV)
            def _(v, s0=s0):
                r = ROWA
                v.tensor_copy(r[0:1, rS:rS + 512], pbS[0:1, :])
                v.tensor_copy(r[0:1, rQ:rQ + 512], pbQ[0:1, :])
                if rms:
                    return v.tensor_scalar(r[0:1, rVar:rVar + 512],
                                           r[0:1, rQ:rQ + 512], 1.0 / D,
                                           float(eps), ALU.mult, ALU.add)
                v.tensor_scalar_mul(r[0:1, rMu:rMu + 512], r[0:1, rS:rS + 512], 1.0 / D)
                v.tensor_scalar_mul(r[0:1, rVar:rVar + 512], r[0:1, rQ:rQ + 512], 1.0 / D)
                v.tensor_mul(r[0:1, rMu2:rMu2 + 512], r[0:1, rMu:rMu + 512],
                             r[0:1, rMu:rMu + 512])
                v.tensor_sub(r[0:1, rVar:rVar + 512], r[0:1, rVar:rVar + 512],
                             r[0:1, rMu2:rMu2 + 512])
                return v.tensor_scalar_add(r[0:1, rVar:rVar + 512],
                                           r[0:1, rVar:rVar + 512], float(eps))
            ev_var = DV.now()

            @P.step("scalar", wait=[ev_var], inc=AC)
            def _(s, s0=s0):
                s.activation(ROWA[0:1, rLn:rLn + 512], ROWA[0:1, rVar:rVar + 512],
                             AF.Ln, bias=0.0, scale=1.0)
                return s.activation(ROWA[0:1, rRstd:rRstd + 512],
                                    ROWA[0:1, rLn:rLn + 512],
                                    AF.Exp, bias=0.0, scale=-0.5)
            ac_rstd = AC.now()


            @P.step("vector", wait=[ac_rstd, PE.now()], inc=DV)
            def _(v, s0=s0):
                r, rb = ROWA, ROWB
                v.tensor_copy(rb[0:1, cRSh:cRSh + 512], r[0:1, rRstd:rRstd + 512])
                v.tensor_sub(rb[0:1, cRSl:cRSl + 512],
                             r[0:1, rRstd:rRstd + 512], rb[0:1, cRSh:cRSh + 512])
                if not rms:
                    v.tensor_mul(r[0:1, rMur:rMur + 512], r[0:1, rMu:rMu + 512],
                                 r[0:1, rRstd:rRstd + 512])
                    v.tensor_copy(rb[0:1, cMUh:cMUh + 512], r[0:1, rMur:rMur + 512])
                    v.tensor_sub(rb[0:1, cMUl:cMUl + 512],
                                 r[0:1, rMur:rMur + 512],
                                 rb[0:1, cMUh:cMUh + 512])
                if save_stats:
                    v.tensor_copy(ROWA[32:33, s0:s0 + 512],
                                  r[0:1, rRstd:rRstd + 512])
                    if not rms:
                        v.tensor_copy(ROWA[96:97, s0:s0 + 512],
                                      r[0:1, rMur:rMur + 512])
                return v.tensor_copy(r[0:1, rRstd:rRstd + 512],
                                     r[0:1, rRstd:rRstd + 512])
            ev_rows = DV.now()

            @P.step("tensor", wait=[ev_rows, rows_ready], inc=PE)
            def _(t, s0=s0):
                rb = ROWB
                ones = rb[0:1, ONES0:ONES0 + 128]
                t.matmul(pbB1[:, :], ones, rb[0:1, cRSh:cRSh + 512],
                         start=True, stop=False)
                t.matmul(pbB1[:, :], ones, rb[0:1, cRSl:cRSl + 512],
                         start=False, stop=True)
                ch2, cl2 = (cRSh, cRSl) if rms else (cMUh, cMUl)
                t.matmul(pbB2[:, :], ones, rb[0:1, ch2:ch2 + 512],
                         start=True, stop=False)
                return t.matmul(pbB2[:, :], ones, rb[0:1, cl2:cl2 + 512],
                                start=False, stop=True)
            bc_done = PE.now()

            for c in range(DC):
                @P.step("vector", wait=[bc_done], inc=DV)
                def _(v, c=c, s0=s0):
                    sl = slice(c * S + s0, c * S + s0 + 512)
                    tmpf = RING[:, ring0:ring0 + 1024].bitcast(F32)
                    if rms:
                        if w_col is None:
                            return v.tensor_tensor(dst[:, sl], src[:, sl],
                                                   pbB1[:, :], ALU.mult)
                        v.tensor_tensor(tmpf[:, 0:512], src[:, sl], pbB1[:, :],
                                        ALU.mult)
                        return v.tensor_scalar_mul(dst[:, sl], tmpf[:, 0:512],
                                                   w_col[:, c:c + 1])
                    v.tensor_tensor(tmpf[:, 0:512], src[:, sl], pbB1[:, :], ALU.mult)
                    return v.tensor_sub(dst[:, sl], tmpf[:, 0:512], pbB2[:, :])
            last = DV.now()
        return last

    def dbg_dump(name, tiles, ready):
        if name not in dbg_out:
            return
        dst = dbg_out[name]
        for ap, col0, ncols in tiles:
            for o in range(0, ncols, 2048):
                w = min(2048, ncols - o)
                if ap.dtype == BF16:
                    @P.step("gpsimd", wait=[(LD, LD.n), ready], inc=LD, inc_k=16)
                    def _(g, ap=ap, col0=col0, o=o, w=w):
                        return g.dma_start(dst[:, col0 + o:col0 + o + w],
                                           ap[:, o:o + w])
                else:
                    @P.step("sync", wait=[ready], inc=ST, inc_k=16)
                    def _(sy, ap=ap, col0=col0, o=o, w=w):
                        return sy.dma_start(dst[:, col0 + o:col0 + o + w],
                                            ap[:, o:o + w])

    fin_ready = a_ready[-1]
    if "R" in phases:
        xn1_done = emit_norm(A, XN, a_ready[-1], rms=False, w_col=None,
                             extra_wait=setup_w)
        dbg_dump("xn1", [(XN[:, 0:DC * S], 0, DC * S)], xn1_done)

        XQC = R4[:, 0 * S:1 * S]
        XQS = R4[:, 1 * S:2 * S]
        XKC = R4[:, 2 * S:3 * S]
        XKS = R4[:, 3 * S:4 * S]
        tw = [gdma(XQC, dram["xq_cos"]), gdma(XQS, dram["xq_sin"]),
              gdma(XKC, dram["xk_cos"]), gdma(XKS, dram["xk_sin"])]
        GM = [R4[:, 4 * S:4 * S + GW], R4[:, 4 * S + GW:4 * S + 2 * GW]]
        wq_sl = [0, 1024]
        wk_sl = [2048, 3072]
        wv_sl = [4096, 5120]
        wg_sl = [6144, 7168]
        wo_sl = [8192, 9216]
        QRAW = 10240
        KRAW = 12288
        T2 = 14336
        VH = 16384
        G8 = [18432 + i * 2048 for i in range(2)]
        GT8 = [22528 + i * 2048 for i in range(2)]
        YB = 26624
        YQ = 28672
        PT = 30720

        wqP_sl = [13056, 14080]
        wkP_sl = [14336, 15360]

        wq_w = [gdma(R3b[:, wq_sl[0]:wq_sl[0] + 1024], dram["wqS"][0])]
        wk_w = [gdma(R3b[:, wk_sl[0]:wk_sl[0] + 1024], dram["wkS"][0])]
        wqP_w = [gdma(R4[:, wqP_sl[0]:wqP_sl[0] + 1024], dram["wqPS"][0])]
        wkP_w = [gdma(R3b[:, wkP_sl[0]:wkP_sl[0] + 1024], dram["wkPS"][0])]
        wv_w = [gdma(R3b[:, wv_sl[0]:wv_sl[0] + 1024], dram["wvS"][0])]
        wg_w = [gdma(R3b[:, wg_sl[0]:wg_sl[0] + 1024], dram["wgS"][0])]
        wo_w = [gdma(R3b[:, wo_sl[0]:wo_sl[0] + 1024], dram["wo"][0:128, :])]
        gm_w = [gdma(GM[0], dram["gmask"][0])]

        prev_tail = xn1_done
        y_free = [None] * SC4
        wo_acc_pending = []
        SUBS = ("proj", "projv", "projvg", "scores", "gn", "full")
        sublevel = SUBS.index(rsub)

        for h in range(min(H1, nheads)):
            par = h & 1
            wq_c, wk_c, wv_c = wq_sl[par], wk_sl[par], wv_sl[par]
            wqP_c, wkP_c = wqP_sl[par], wkP_sl[par]
            wqP_rdy, wkP_rdy = wqP_w[h], wkP_w[h]
            wg_c, wo_c = wg_sl[par], wo_sl[par]
            gm_c = GM[par]
            wq_rdy, wk_rdy, wv_rdy = wq_w[h], wk_w[h], wv_w[h]
            wg_rdy, wo_rdy = wg_w[h], wo_w[h]
            gm_rdy = gm_w[h]

            def qk_proj(raw_col, wcol, wPc, wPt, wrdy, wPrdy, cosT, sinT, free2):
                evs = []
                for sc in range(SC4):
                    pbA = PS[sc & 1]
                    pbB = PS[2 + (sc & 1)]
                    s0 = sc * 512
                    w = [wrdy, wPrdy, xn1_done]
                    w.append(evs[-2] if len(evs) >= 2 else free2[sc & 1])

                    @P.step("tensor", wait=w, inc=PE)
                    def _(t, pbA=pbA, pbB=pbB, s0=s0, wcol=wcol, wPc=wPc):
                        for c in range(DC):
                            t.matmul(pbA[:, :],
                                     R3b[:, wcol + c * 128:wcol + (c + 1) * 128],
                                     XN[:, c * S + s0:c * S + s0 + 512],
                                     start=(c == 0), stop=(c == DC - 1))
                        mm = None
                        for c in range(DC):
                            mm = t.matmul(pbB[:, :],
                                          wPt[:, wPc + c * 128:wPc + (c + 1) * 128],
                                          XN[:, c * S + s0:c * S + s0 + 512],
                                          start=(c == 0), stop=(c == DC - 1))
                        return mm
                    mm_ev = PE.now()

                    @P.step("vector", wait=[mm_ev, tw[-1]], inc=DV)
                    def _(v, pbA=pbA, pbB=pbB, s0=s0, raw_col=raw_col,
                          cosT=cosT, sinT=sinT):
                        rc = R3b[:, raw_col + s0:raw_col + s0 + 512]
                        t2 = R3b[:, YQ + s0:YQ + s0 + 512]
                        v.tensor_tensor(t2, pbB[:, :], sinT[:, s0:s0 + 512], ALU.mult)
                        v.tensor_tensor(rc, pbA[:, :], cosT[:, s0:s0 + 512], ALU.mult)
                        return v.tensor_add(rc, rc, t2)
                    evs.append(DV.now())
                return evs

            q_evs = qk_proj(QRAW, wq_c, wqP_c, R4, wq_rdy, wqP_rdy, XQC, XQS,
                            (prev_tail, prev_tail))
            k_evs = qk_proj(KRAW, wk_c, wkP_c, R3b, wk_rdy, wkP_rdy, XKC, XKS,
                            (q_evs[2], q_evs[3]))
            qx_done, kx_done = q_evs[-1], k_evs[-1]
            if h == 0:
                dbg_dump("qx0", [(R3b[:, QRAW:QRAW + S], 0, S)], qx_done)
                dbg_dump("kx0", [(R3b[:, KRAW:KRAW + S], 0, S)], kx_done)

            if sublevel < 1:
                continue
            v_evs = []
            for tc in range(NT):
                pb = PS[tc & 1]
                w = [wv_rdy]
                w.append(v_evs[-2] if len(v_evs) >= 2
                         else k_evs[2 + (tc & 1)])

                @P.step("tensor", wait=w, inc=PE)
                def _(t, pb=pb, tc=tc, wv_c=wv_c):
                    mm = None
                    for c in range(DC):
                        mm = t.matmul(pb[:, 0:128],
                                      XN[:, c * S + tc * 128:c * S + (tc + 1) * 128],
                                      R3b[:, wv_c + c * 128:wv_c + (c + 1) * 128],
                                      start=(c == 0), stop=(c == DC - 1))
                    return mm
                mm_done = PE.now()

                @P.step("scalar", wait=[mm_done], inc=AC)
                def _(s, pb=pb, tc=tc):
                    return s.copy(R3b[:, VH + tc * 128:VH + (tc + 1) * 128],
                                  pb[:, 0:128])
                v_evs.append(AC.now())
            v_done = v_evs[-1]
            if h == 0:
                dbg_dump("v0", [(R3b[:, VH:VH + NT * 128], 0, NT * 128)], v_done)

            if sublevel < 2:
                continue
            gcol = G8[par]
            g_evs = []
            for sc in range(SC4):
                pb = PS[2 + (sc & 1)]
                s0 = sc * 512
                w = [wg_rdy]
                w.append(g_evs[-2] if len(g_evs) >= 2
                         else k_evs[2 + (sc & 1)])

                @P.step("tensor", wait=w, inc=PE)
                def _(t, pb=pb, s0=s0, wg_c=wg_c):
                    mm = None
                    for c in range(DC):
                        mm = t.matmul(pb[:, :],
                                      R3b[:, wg_c + c * 128:wg_c + (c + 1) * 128],
                                      XN[:, c * S + s0:c * S + s0 + 512],
                                      start=(c == 0), stop=(c == DC - 1))
                    return mm
                mm_done = PE.now()

                @P.step("scalar", wait=[mm_done], inc=AC)
                def _(s, pb=pb, s0=s0, gcol=gcol):
                    return s.activation(R3b[:, gcol + s0:gcol + s0 + 512], pb[:, :],
                                        AF.Silu, bias=0.0, scale=1.0)
                g_evs.append(AC.now())
            g_done = g_evs[-1]
            if h == 0:
                dbg_dump("gate0", [(R3b[:, G8[0]:G8[0] + S], 0, S)], g_done)

            if sublevel < 3:
                continue
            pv_started = [False] * SC4
            last_bank = {0: v_evs[-2], 1: v_evs[-1]}
            pt_ring = {}
            pv_ev = [None] * SC4
            for tc in range(NT):
                sc_lo = (tc * 128) // 512
                for sc in range(sc_lo, SC4):
                    b = (tc + sc) & 1
                    pb = PS[b]
                    s0 = sc * 512
                    ring = (tc + sc) & 3
                    pt_col = PT + ring * 512

                    @P.step("tensor", wait=[q_evs[sc], k_evs[tc // 4],
                                            last_bank[b]], inc=PE)
                    def _(t, pb=pb, s0=s0, tc=tc):
                        return t.matmul(pb[:, :],
                                        R3b[:, KRAW + tc * 128:KRAW + (tc + 1) * 128],
                                        R3b[:, QRAW + s0:QRAW + s0 + 512],
                                        start=True, stop=True)
                    mm_done = PE.now()

                    mw = [mm_done, gm_rdy]
                    if ring in pt_ring:
                        mw.append(pt_ring[ring])

                    @P.step("vector", wait=mw, inc=DV)
                    def _(v, pb=pb, pt_col=pt_col, tc=tc, s0=s0, gm_c=gm_c):
                        u0 = 384 + s0 - tc * 128
                        return v.tensor_tensor(R3b[:, pt_col:pt_col + 512], pb[:, :],
                                               gm_c[:, u0:u0 + 512], ALU.mult)
                    mask_done = DV.now()
                    last_bank[b] = mask_done

                    tc_last = 4 * sc + 3
                    pvw = [mask_done, v_evs[tc]]
                    if not pv_started[sc] and y_free[sc] is not None:
                        pvw.append(y_free[sc])

                    @P.step("tensor", wait=pvw, inc=PE)
                    def _(t, sc=sc, tc=tc, pt_col=pt_col, tc_last=tc_last,
                          first=(not pv_started[sc])):
                        return t.matmul(PS[4 + sc][:, :],
                                        R3b[:, VH + tc * 128:VH + (tc + 1) * 128],
                                        R3b[:, pt_col:pt_col + 512],
                                        start=first, stop=(tc == tc_last))
                    pv_started[sc] = True
                    pt_ring[ring] = PE.now()
                    if tc == tc_last:
                        pv_ev[sc] = PE.now()
            pv_done = PE.now()
            if h == 0:
                pass

            if sublevel < 4:
                for sc in range(SC4):
                    @P.step("scalar", wait=[pv_done], inc=AC)
                    def _(s, sc=sc):
                        return s.copy(R3b[:, YB + sc * 512:YB + sc * 512 + 512],
                                      PS[4 + sc][:, :])
                continue
            gtcol = GT8[par]
            gn_evs = []
            for sc in range(SC4):
                s0 = sc * 512

                @P.step("scalar", wait=[pv_ev[sc]], inc=AC)
                def _(s, sc=sc, s0=s0):
                    return s.copy(R3b[:, YB + s0:YB + s0 + 512], PS[4 + sc][:, :])
                y_ev = AC.now()
                y_free[sc] = y_ev

                @P.step("vector", wait=[y_ev], inc=DV)
                def _(v, s0=s0):
                    return v.tensor_mul(R3b[:, YQ + s0:YQ + s0 + 512],
                                        R3b[:, YB + s0:YB + s0 + 512],
                                        R3b[:, YB + s0:YB + s0 + 512])
                ysq_ev = DV.now()

                stw = [ysq_ev]
                if sc == 0:
                    stw += [last_bank[0], last_bank[1]]

                @P.step("tensor", wait=stw, inc=PE)
                def _(t, s0=s0):
                    t.matmul(PS[0][0:1, :], ONESC[:, 0:1],
                             R3b[:, YB + s0:YB + s0 + 512], start=True, stop=True)
                    return t.matmul(PS[1][0:1, :], ONESC[:, 0:1],
                                    R3b[:, YQ + s0:YQ + s0 + 512], start=True, stop=True)
                st_done = PE.now()

                @P.step("vector", wait=[st_done], inc=DV)
                def _(v, s0=s0):
                    r = ROWA
                    v.tensor_scalar_mul(r[0:1, rMu:rMu + 512], PS[0][0:1, :], 1.0 / 128)
                    v.tensor_scalar_mul(r[0:1, rVar:rVar + 512], PS[1][0:1, :], 1.0 / 128)
                    v.tensor_mul(r[0:1, rMu2:rMu2 + 512], r[0:1, rMu:rMu + 512],
                                 r[0:1, rMu:rMu + 512])
                    v.tensor_sub(r[0:1, rVar:rVar + 512], r[0:1, rVar:rVar + 512],
                                 r[0:1, rMu2:rMu2 + 512])
                    return v.tensor_scalar_add(r[0:1, rVar:rVar + 512],
                                               r[0:1, rVar:rVar + 512], LN_EPS)
                var_ev = DV.now()

                @P.step("scalar", wait=[var_ev], inc=AC)
                def _(s, s0=s0):
                    s.activation(ROWA[0:1, rLn:rLn + 512], ROWA[0:1, rVar:rVar + 512],
                                 AF.Ln, bias=0.0, scale=1.0)
                    return s.activation(ROWA[0:1, rRstd:rRstd + 512],
                                        ROWA[0:1, rLn:rLn + 512],
                                        AF.Exp, bias=0.0, scale=-0.5)
                rstd_ev = AC.now()

                @P.step("vector", wait=[rstd_ev], inc=DV)
                def _(v, s0=s0):
                    r, rb = ROWA, ROWB
                    v.tensor_copy(rb[0:1, cRSh:cRSh + 512],
                                  r[0:1, rRstd:rRstd + 512])
                    v.tensor_sub(rb[0:1, cRSl:cRSl + 512],
                                 r[0:1, rRstd:rRstd + 512],
                                 rb[0:1, cRSh:cRSh + 512])
                    v.tensor_mul(r[0:1, rMur:rMur + 512], r[0:1, rMu:rMu + 512],
                                 r[0:1, rRstd:rRstd + 512])
                    v.tensor_copy(rb[0:1, cMUh:cMUh + 512],
                                  r[0:1, rMur:rMur + 512])
                    return v.tensor_sub(rb[0:1, cMUl:cMUl + 512],
                                        r[0:1, rMur:rMur + 512],
                                        rb[0:1, cMUh:cMUh + 512])
                rows_ev = DV.now()

                bcw = [rows_ev]
                if sc == 0:
                    bcw += [g_evs[2], g_evs[3]]

                @P.step("tensor", wait=bcw, inc=PE)
                def _(t, s0=s0):
                    rb = ROWB
                    ones = rb[0:1, ONES0:ONES0 + 128]
                    t.matmul(PS[2][:, :], ones, rb[0:1, cRSh:cRSh + 512],
                             start=True, stop=False)
                    t.matmul(PS[2][:, :], ones, rb[0:1, cRSl:cRSl + 512],
                             start=False, stop=True)
                    t.matmul(PS[3][:, :], ones, rb[0:1, cMUh:cMUh + 512],
                             start=True, stop=False)
                    return t.matmul(PS[3][:, :], ones, rb[0:1, cMUl:cMUl + 512],
                                    start=False, stop=True)
                bc_done = PE.now()

                @P.step("vector", wait=[bc_done, g_evs[sc]], inc=DV)
                def _(v, s0=s0, h=h, gcol=gcol, gtcol=gtcol):
                    tmp = R3b[:, PT:PT + 1024].bitcast(F32)
                    yb = R3b[:, YB + s0:YB + s0 + 512]
                    v.tensor_tensor(tmp[:, 0:512], yb, PS[2][:, :], ALU.mult)
                    v.tensor_sub(tmp[:, 0:512], tmp[:, 0:512], PS[3][:, :])
                    v.tensor_scalar(tmp[:, 0:512], tmp[:, 0:512],
                                    GNWC[:, h:h + 1], GNBC[:, h:h + 1],
                                    ALU.mult, ALU.add)
                    return v.tensor_mul(R3b[:, gtcol + s0:gtcol + s0 + 512],
                                        tmp[:, 0:512],
                                        R3b[:, gcol + s0:gcol + s0 + 512])
                gn_evs.append(DV.now())
            gated_done = gn_evs[-1]
            if par == 0:
                gn_evs_even = gn_evs
            if h == 0:
                dbg_dump("y0", [(R3b[:, YB:YB + S], 0, S)], gn_evs[-1])
                dbg_dump("gn0", [(R3b[:, GT8[0]:GT8[0] + S], 0, S)], gn_evs[-1])
            if h == 1:
                dbg_dump("gn1", [(R3b[:, GT8[1]:GT8[1] + S], 0, S)], gn_evs[-1])

            if par == 1:
                wo_last = {0: gn_evs[-1], 1: gn_evs[-1]}
                for jc in range(DC):
                    for sc in range(SC4):
                        s0 = sc * 512
                        b = (jc + sc) & 1
                        pb = PS[b]

                        @P.step("tensor", wait=[gn_evs_even[sc], gn_evs[sc],
                                                wo_rdy, wo_last[b]], inc=PE)
                        def _(t, pb=pb, jc=jc, s0=s0):
                            t.matmul(pb[:, :],
                                     R3b[:, wo_sl[0] + jc * 128:wo_sl[0] + (jc + 1) * 128],
                                     R3b[:, GT8[0] + s0:GT8[0] + s0 + 512],
                                     start=True, stop=False)
                            return t.matmul(pb[:, :],
                                            R3b[:, wo_sl[1] + jc * 128:wo_sl[1] + (jc + 1) * 128],
                                            R3b[:, GT8[1] + s0:GT8[1] + s0 + 512],
                                            start=False, stop=True)
                        mm_done = PE.now()
                        wo_mm_last = mm_done

                        if False and h == 1 and jc == 0 and s0 == 0 and "wops" in dbg_out:
                            @P.step("scalar", wait=[mm_done], inc=AC)
                            def _(s, pb=pb):
                                return s.copy(R3b[:, YQ:YQ + 1024].bitcast(F32),
                                              pb[:, :])
                            dbg_dump("wops",
                                     [(R3b[:, YQ:YQ + 1024].bitcast(F32), 0, 512)],
                                     AC.now())

                        @P.step("vector", wait=[mm_done], inc=DV)
                        def _(v, pb=pb, jc=jc, s0=s0):
                            sl = slice(jc * S + s0, jc * S + s0 + 512)
                            return v.tensor_add(A[:, sl], A[:, sl], pb[:, :])
                        wo_last[b] = DV.now()
                wo_acc_pending.append(DV.now())
                prev_tail = wo_acc_pending[-1]
            else:
                prev_tail = gn_evs[-1]

            if h + 1 < min(H1, nheads):
                np_ = (h + 1) & 1
                gate = [qx_done, kx_done, v_done, g_done]
                if par == 1:
                    gate.append(wo_mm_last)
                wq_w.append(gdma(R3b[:, wq_sl[np_]:wq_sl[np_] + 1024],
                                 dram["wqS"][h + 1], wait=gate))
                wk_w.append(gdma(R3b[:, wk_sl[np_]:wk_sl[np_] + 1024],
                                 dram["wkS"][h + 1], wait=gate))
                wqP_w.append(gdma(R4[:, wqP_sl[np_]:wqP_sl[np_] + 1024],
                                  dram["wqPS"][h + 1], wait=gate))
                wkP_w.append(gdma(R3b[:, wkP_sl[np_]:wkP_sl[np_] + 1024],
                                  dram["wkPS"][h + 1], wait=gate))
                wv_w.append(gdma(R3b[:, wv_sl[np_]:wv_sl[np_] + 1024],
                                 dram["wvS"][h + 1], wait=gate))
                wg_w.append(gdma(R3b[:, wg_sl[np_]:wg_sl[np_] + 1024],
                                 dram["wgS"][h + 1], wait=gate))
                wo_w.append(gdma(R3b[:, wo_sl[np_]:wo_sl[np_] + 1024],
                                 dram["wo"][(h + 1) * 128:(h + 2) * 128, :],
                                 wait=gate))

                gm_w.append(gdma(GM[np_], dram["gmask"][h + 1], wait=gate))

        if wo_acc_pending:
            r1_done = wo_acc_pending[-1]
            dbg_dump("r1", [(A[:, 0:DC * S], 0, DC * S)], r1_done)
            fin_ready = r1_done
        else:
            fin_ready = (DV, DV.n)


    if "F" in phases:
        r1_ready = fin_ready

        RCOS = 0
        RSIN = 2048
        QSL = 4096
        KSL = 6144
        QW = 10240
        KW = 11264
        PTB2 = 12288
        VSL = [14336, 16384]
        VG = [18432, 22528]
        FT2 = 26624
        PNA = 26624
        PNB = 28672
        PTA = 30720
        PTB = 26624

        pbS, pbQ, pbB1, pbB2 = PS[4], PS[5], PS[6], PS[7]
        xn2_parts = []
        for sc in range(SC4):
            s0 = sc * 512
            dv_tiles = []
            for c in range(DC):
                bcp = R4[:, c * 512: c * 512 + 512]
                bsq = R4[:, (8 + c) * 512: (8 + c) * 512 + 512]

                @P.step("vector", wait=[r1_ready, PE.now(), AC.now()], inc=DV)
                def _(v, bcp=bcp, bsq=bsq, c=c, s0=s0):
                    v.tensor_copy(bcp, A[:, c * S + s0: c * S + s0 + 512])
                    return v.tensor_mul(bsq, bcp, bcp)
                dv_tiles.append(DV.now())
            for c in range(DC):
                bcp = R4[:, c * 512: c * 512 + 512]
                bsq = R4[:, (8 + c) * 512: (8 + c) * 512 + 512]

                @P.step("tensor", wait=[dv_tiles[c]],
                        inc=(PE if c == DC - 1 else None))
                def _(t, bcp=bcp, bsq=bsq, c=c):
                    t.matmul(pbS[0:1, :], ONESC[:, 0:1], bcp,
                             start=(c == 0), stop=(c == DC - 1))
                    return t.matmul(pbQ[0:1, :], ONESC[:, 0:1], bsq,
                                    start=(c == 0), stop=(c == DC - 1))
            stats_done = PE.now()

            @P.step("vector", wait=[stats_done, AC.now(), PE.now()], inc=DV)
            def _(v, s0=s0):
                r = ROWA
                v.tensor_copy(r[0:1, rS:rS + 512], pbS[0:1, :])
                v.tensor_copy(r[0:1, rQ:rQ + 512], pbQ[0:1, :])
                v.tensor_scalar_mul(r[0:1, rMu:rMu + 512], r[0:1, rS:rS + 512],
                                    1.0 / D)
                v.tensor_scalar_mul(r[0:1, rVar:rVar + 512], r[0:1, rQ:rQ + 512],
                                    1.0 / D)
                v.tensor_mul(r[0:1, rMu2:rMu2 + 512], r[0:1, rMu:rMu + 512],
                             r[0:1, rMu:rMu + 512])
                v.tensor_sub(r[0:1, rVar:rVar + 512], r[0:1, rVar:rVar + 512],
                             r[0:1, rMu2:rMu2 + 512])
                return v.tensor_scalar_add(r[0:1, rVar:rVar + 512],
                                           r[0:1, rVar:rVar + 512], LN_EPS)
            ev_var = DV.now()

            @P.step("scalar", wait=[ev_var], inc=AC)
            def _(s, s0=s0):
                s.activation(ROWA[0:1, rLn:rLn + 512], ROWA[0:1, rVar:rVar + 512],
                             AF.Ln, bias=0.0, scale=1.0)
                return s.activation(ROWA[0:1, rRstd:rRstd + 512],
                                    ROWA[0:1, rLn:rLn + 512],
                                    AF.Exp, bias=0.0, scale=-0.5)
            ac1 = AC.now()

            @P.step("vector", wait=[ac1, PE.now()], inc=DV)
            def _(v, s0=s0):
                r = ROWA
                v.tensor_copy(r[32:33, s0:s0 + 512], r[0:1, rRstd:rRstd + 512])
                v.tensor_mul(r[96:97, s0:s0 + 512], r[0:1, rMu:rMu + 512],
                             r[0:1, rRstd:rRstd + 512])
                v.tensor_mul(r[0:1, rVar:rVar + 512], r[0:1, rRstd:rRstd + 512],
                             r[0:1, rRstd:rRstd + 512])
                return v.tensor_scalar(r[0:1, rVar:rVar + 512],
                                       r[0:1, rVar:rVar + 512],
                                       -float(LN_EPS), 1.0 + float(RMS_EPS),
                                       ALU.mult, ALU.add)
            ev2 = DV.now()

            @P.step("scalar", wait=[ev2], inc=AC)
            def _(s, s0=s0):
                s.activation(ROWA[0:1, rQ:rQ + 512], ROWA[0:1, rVar:rVar + 512],
                             AF.Ln, bias=0.0, scale=1.0)
                return s.activation(ROWA[0:1, rQ:rQ + 512], ROWA[0:1, rQ:rQ + 512],
                                    AF.Exp, bias=0.0, scale=-0.5)
            ac2 = AC.now()

            @P.step("vector", wait=[ac2, PE.now()], inc=DV)
            def _(v, s0=s0):
                r, rb = ROWA, ROWB
                v.tensor_mul(r[0:1, rS:rS + 512], r[0:1, rS:rS + 512],
                             r[0:1, rQ:rQ + 512])
                v.tensor_mul(r[0:1, rQ:rQ + 512], r[0:1, rMu:rMu + 512],
                             r[0:1, rS:rS + 512])
                v.tensor_copy(rb[0:1, cRSh:cRSh + 512], r[0:1, rS:rS + 512])
                v.tensor_sub(rb[0:1, cRSl:cRSl + 512], r[0:1, rS:rS + 512],
                             rb[0:1, cRSh:cRSh + 512])
                v.tensor_copy(rb[0:1, cMUh:cMUh + 512], r[0:1, rQ:rQ + 512])
                return v.tensor_sub(rb[0:1, cMUl:cMUl + 512], r[0:1, rQ:rQ + 512],
                                    rb[0:1, cMUh:cMUh + 512])
            ev3 = DV.now()

            @P.step("tensor", wait=[ev3], inc=PE)
            def _(t, s0=s0):
                rb = ROWB
                ones = rb[0:1, ONES0:ONES0 + 128]
                t.matmul(pbB1[:, :], ones, rb[0:1, cRSh:cRSh + 512],
                         start=True, stop=False)
                t.matmul(pbB1[:, :], ones, rb[0:1, cRSl:cRSl + 512],
                         start=False, stop=True)
                t.matmul(pbB2[:, :], ones, rb[0:1, cMUh:cMUh + 512],
                         start=True, stop=False)
                return t.matmul(pbB2[:, :], ones, rb[0:1, cMUl:cMUl + 512],
                                start=False, stop=True)
            bcf = PE.now()

            for c in range(DC):
                @P.step("vector", wait=[bcf], inc=DV)
                def _(v, c=c, s0=s0):
                    sl = slice(c * S + s0, c * S + s0 + 512)
                    tmpf = R4[:, 0:1024].bitcast(F32)
                    v.tensor_tensor(tmpf[:, 0:512], A[:, sl], pbB1[:, :], ALU.mult)
                    v.tensor_sub(tmpf[:, 0:512], tmpf[:, 0:512], pbB2[:, :])
                    return v.tensor_scalar_mul(XN[:, sl], tmpf[:, 0:512],
                                               N1WC[:, c:c + 1])
            xn2_parts.append(DV.now())
        xn2_done = xn2_parts[-1]
        dbg_dump("xn2", [(XN[:, 0:DC * S], 0, DC * S)], xn2_done)

        ftw = [gdma(R3b[:, RCOS:RCOS + S], dram["rope_cos"], wait=[xn2_done]),
               gdma(R3b[:, RSIN:RSIN + S], dram["rope_sin"], wait=[xn2_done])]
        cw2 = gdma(ROWA[0:1, 0:1024], dram["vb_row"], wait=[xn2_done])

        @P.step("vector", wait=[cw2, PE.now()], inc=DV)
        def _(v):
            v.tensor_copy(ROWB[0:1, 0:1024], ROWA[0:1, 0:1024])
            return v.tensor_sub(ROWB[0:1, 1024:2048], ROWA[0:1, 0:1024],
                                ROWB[0:1, 0:1024])
        vb_ready = DV.now()

        def vslab_dma(slot, g, wait=()):
            d4 = R3b[:, slot:slot + 2048].rearrange("p (c two j) -> p c two j",
                                                    two=2, j=128)
            gdma(d4[:, :, 0, :], dram["qkvS"][16 + 2 * g], wait=wait)
            return gdma(d4[:, :, 1, :], dram["qkvS"][17 + 2 * g], wait=wait)

        def qkslab_dma(hp, wait=()):
            gdma(R3b[:, QW:QW + 1024], dram["qkvS"][hp], wait=wait)
            return gdma(R3b[:, KW:KW + 1024], dram["qkvS"][8 + hp], wait=wait)

        v_w = [vslab_dma(VSL[0], 0, wait=[xn2_done])]
        qk_w = [qkslab_dma(0, wait=[xn2_done])]

        vg_done = [None] * 4

        pair_free = [[r1_ready, xn2_done], [r1_ready, xn2_done]]
        attn_evs = []

        for hp in range(8):
            par = hp & 1
            g = hp // 2
            qsl, ksl = QSL, KSL

            if hp % 2 == 0 and vg_done[g] is None:
                vgc = VG[g & 1]
                v_evs = []
                for tc in range(NT):
                    pb = PS[tc & 1]
                    w = [v_w[g], vb_ready]
                    if len(v_evs) >= 2:
                        w.append(v_evs[-2])
                    else:
                        w += pair_free[0]

                    @P.step("tensor", wait=w, inc=PE)
                    def _(t, pb=pb, tc=tc, vs=VSL[g & 1], g=g):
                        mm = None
                        for c in range(DC):
                            mm = t.matmul(pb[:, 0:256],
                                          XN[:, c * S + tc * 128:c * S + (tc + 1) * 128],
                                          R3b[:, vs + c * 256:vs + (c + 1) * 256],
                                          start=(c == 0), stop=False)
                        ones = ROWB[0:1, ONES0:ONES0 + 128]
                        t.matmul(pb[:, 0:256], ones,
                                 ROWB[0:1, g * 256:(g + 1) * 256],
                                 start=False, stop=False)
                        return t.matmul(pb[:, 0:256], ones,
                                        ROWB[0:1, 1024 + g * 256:1024 + (g + 1) * 256],
                                        start=False, stop=True)
                    mm_done = PE.now()

                    @P.step("scalar", wait=[mm_done], inc=AC)
                    def _(s, pb=pb, tc=tc, vgc=vgc):
                        return s.copy(R3b[:, vgc + tc * 256:vgc + (tc + 1) * 256],
                                      pb[:, 0:256])
                    v_evs.append(AC.now())
                vg_done[g] = v_evs[-1]
                if g == 0:
                    dbg_dump("vin0", [(R3b[:, VG[0]:VG[0] + NT * 256], 0, NT * 256)],
                             vg_done[0])
            vgc = VG[g & 1]
            vready = vg_done[g]

            def qk_proj_f(wslot, dst, biascol, wrdy):
                raw_evs = []
                mm_evs = []
                for sc in range(SC4):
                    pb = PS[sc]
                    s0 = sc * 512
                    w = [wrdy]
                    if dst == QSL and sc < 2:
                        w.append(v_evs[-2 + sc] if vg_done[g] is not None
                                 else pair_free[0][0])
                        w += pair_free[0]
                    elif dst == QSL:
                        w += pair_free[0]
                    else:
                        w.append(q_raw[sc])

                    @P.step("tensor", wait=w, inc=PE)
                    def _(t, pb=pb, s0=s0, wslot=wslot):
                        mm = None
                        for c in range(DC):
                            mm = t.matmul(pb[:, :],
                                          R3b[:, wslot + c * 128:wslot + (c + 1) * 128],
                                          XN[:, c * S + s0:c * S + s0 + 512],
                                          start=(c == 0), stop=(c == DC - 1))
                        return mm
                    mm_evs.append(PE.now())

                    @P.step("scalar", wait=[mm_evs[-1]], inc=AC)
                    def _(s, pb=pb, s0=s0, dst=dst, biascol=biascol):
                        return s.activation(R3b[:, dst + s0:dst + s0 + 512], pb[:, :],
                                            AF.Identity,
                                            bias=QKVBC[:, biascol:biascol + 1],
                                            scale=1.0)
                    raw_evs.append(AC.now())

                comb_evs = []
                for sc in range(SC4):
                    ppb = PS[4 + (sc & 1)]
                    s0 = sc * 512
                    w = [raw_evs[sc]]
                    if len(comb_evs) >= 2:
                        w.append(comb_evs[-2])
                    elif dst == QSL:
                        w += pair_free[1]
                    else:
                        w.append(q_comb[2 + (sc & 1)])

                    @P.step("tensor", wait=w, inc=PE)
                    def _(t, ppb=ppb, s0=s0, dst=dst):
                        return t.matmul(ppb[:, :], CPERM64[:, :],
                                        R3b[:, dst + s0:dst + s0 + 512],
                                        start=True, stop=True)
                    perm_done = PE.now()

                    @P.step("vector", wait=[perm_done], inc=DV)
                    def _(v, ppb=ppb, s0=s0, dst=dst):
                        rc = R3b[:, dst + s0:dst + s0 + 512]
                        t2 = R3b[:, FT2 + s0:FT2 + s0 + 512]
                        v.tensor_tensor(t2, ppb[:, :],
                                        R3b[:, RSIN + s0:RSIN + s0 + 512], ALU.mult)
                        v.tensor_tensor(rc, rc, R3b[:, RCOS + s0:RCOS + s0 + 512],
                                        ALU.mult)
                        return v.tensor_add(rc, rc, t2)
                    comb_evs.append(DV.now())
                return raw_evs, mm_evs, comb_evs

            q_raw = None
            q_raw, q_mm, q_comb = qk_proj_f(QW, qsl, hp, qk_w[hp])
            k_raw, k_mm, k_comb = qk_proj_f(KW, ksl, 8 + hp, qk_w[hp])
            q_done, k_done = q_comb[-1], k_comb[-1]
            if hp == 0:
                dbg_dump("qr0", [(R3b[:, QSL:QSL + S], 0, S)], q_done)
                dbg_dump("kr0", [(R3b[:, KSL:KSL + S], 0, S)], k_done)

            if hp + 1 < 8:
                qk_w.append(qkslab_dma(hp + 1, wait=[q_mm[-1], k_mm[-1]]))
                if hp % 2 == 1 and hp // 2 + 1 < 4:
                    v_w.append(vslab_dma(VSL[(g + 1) & 1], g + 1,
                                         wait=[q_done, k_done]))

            voff = 128 * (hp % 2)
            SETP = [PB0, PB1]
            PNH = [PNA, PNB]
            PTH = [PTA, PTB2]
            scm, rmv, expv, recv, dgv, tmm = {}, {}, {}, {}, {}, {}
            evac, pvm, pvev = {}, {}, {}
            CS = 2048

            def emit_sc(si, h):
                sb0 = si * 128
                w = [q_comb[si // 4], k_done]
                if si > 0:
                    w += [evac[(si - 1, h)], pvev[(si - 1, h)]]
                else:
                    w += pair_free[h]

                @P.step("tensor", wait=w, inc=PE)
                def _(t, sb0=sb0, h=h):
                    mm = None
                    for tc4 in range(4):
                        mm = t.matmul(
                            SETP[h][:, tc4 * 512:(tc4 + 1) * 512],
                            R3b[h * 64:h * 64 + 64, qsl + sb0:qsl + sb0 + 128],
                            R3b[h * 64:h * 64 + 64,
                                ksl + tc4 * 512:ksl + (tc4 + 1) * 512],
                            start=True, stop=True)
                    return mm
                scm[(si, h)] = PE.now()

            def emit_rm(si, h):
                c0 = CS + 16 * (si & 1) + 8 * h

                @P.step("vector", wait=[scm[(si, h)]], inc=DV)
                def _(v, h=h, c0=c0):
                    return v.reduce_max(ROWA[:, c0:c0 + 1], SETP[h][:, 0:2048],
                                        axis=AX.X)
                rmv[(si, h)] = DV.now()

            def emit_exp(si, h):
                c0 = CS + 16 * (si & 1) + 8 * h

                @P.step("scalar", wait=[rmv[(si, h)]], inc=AC)
                def _(s, c0=c0):
                    return s.activation(ROWA[:, c0 + 1:c0 + 2], ROWA[:, c0:c0 + 1],
                                        AF.Identity, bias=0.0, scale=-0.125)
                negv = AC.now()

                w = [negv] if si == 0 else [negv, tmm[(si - 1, h)]]

                @P.step("scalar", wait=w, inc=AC)
                def _(s, h=h, c0=c0):
                    return s.activation(R3b[:, PNH[h]:PNH[h] + 2048],
                                        SETP[h][:, 0:2048], AF.Exp,
                                        bias=ROWA[:, c0 + 1:c0 + 2], scale=0.125,
                                        accum_out=ROWA[:, c0 + 2:c0 + 3])
                expv[(si, h)] = AC.now()

                @P.step("scalar", wait=[expv[(si, h)]], inc=AC)
                def _(s, c0=c0):
                    s.activation(ROWA[:, c0 + 5:c0 + 6], ROWA[:, c0:c0 + 1],
                                 AF.Identity, bias=0.0, scale=1.0)
                    s.activation(ROWA[:, c0 + 3:c0 + 4], ROWA[:, c0 + 2:c0 + 3],
                                 AF.Ln, bias=0.0, scale=1.0)
                    return s.activation(ROWA[:, c0 + 4:c0 + 5],
                                        ROWA[:, c0 + 3:c0 + 4],
                                        AF.Exp, bias=0.0, scale=-1.0)
                recv[(si, h)] = AC.now()

            def emit_diag(si, h):
                c0 = CS + 16 * (si & 1) + 8 * h
                w = [recv[(si, h)]]
                if si > 0:
                    w.append(tmm[(si - 1, h)])

                @P.step("vector", wait=w, inc=DV)
                def _(v, h=h, c0=c0):
                    return v.tensor_scalar_mul(ROWB[:, h * 128:h * 128 + 128],
                                               CIDENT[:, :],
                                               ROWA[:, c0 + 4:c0 + 5])
                dgv[(si, h)] = DV.now()

            def emit_T(si, h):
                @P.step("tensor", wait=[expv[(si, h)], dgv[(si, h)]], inc=PE)
                def _(t, h=h):
                    mm = None
                    for tb in range(NT):
                        mm = t.matmul(
                            SETP[h][:, tb * 128:(tb + 1) * 128],
                            R3b[:, PNH[h] + tb * 128:PNH[h] + (tb + 1) * 128],
                            ROWB[:, h * 128:h * 128 + 128],
                            start=True, stop=True)
                    return mm
                tmm[(si, h)] = PE.now()

            def emit_evac(si, h):
                if h == 0:
                    @P.step("scalar", wait=[tmm[(si, h)]], inc=AC)
                    def _(s, h=h):
                        last = None
                        for q4 in range(4):
                            last = s.copy(
                                R3b[:, PTH[h] + q4 * 512:PTH[h] + (q4 + 1) * 512],
                                SETP[h][:, q4 * 512:(q4 + 1) * 512])
                        return last
                    evac[(si, h)] = AC.now()
                else:
                    @P.step("vector", wait=[tmm[(si, h)]], inc=DV)
                    def _(v, h=h):
                        last = None
                        for q4 in range(4):
                            last = v.tensor_copy(
                                R3b[:, PTH[h] + q4 * 512:PTH[h] + (q4 + 1) * 512],
                                SETP[h][:, q4 * 512:(q4 + 1) * 512])
                        return last
                    evac[(si, h)] = DV.now()

            def emit_pv(si, h):
                w = [evac[(si, h)], vready]
                if si > 0:
                    w.append(pvev[(si - 1, h)])

                @P.step("tensor", wait=w, inc=PE)
                def _(t, h=h, vgc=vgc, voff=voff):
                    mm = None
                    po = SETP[h][0:64, 0:128] if h == 0 else SETP[h][64:128, 0:128]
                    tp = (0, 0) if h == 0 else (0, 64)
                    for tc in range(NT):
                        mm = t.matmul(po,
                                      R3b[:, vgc + tc * 256 + voff + h * 64:
                                          vgc + tc * 256 + voff + h * 64 + 64],
                                      R3b[:, PTH[h] + tc * 128:PTH[h] + (tc + 1) * 128],
                                      start=(tc == 0), stop=(tc == NT - 1),
                                      tile_position=tp)
                    return mm
                pvm[(si, h)] = PE.now()

            def emit_pvev(si, h):
                sb0 = si * 128

                @P.step("vector", wait=[pvm[(si, h)]], inc=DV)
                def _(v, sb0=sb0, hp=hp, h=h):
                    col = hp * 2048 + sb0
                    if h == 0:
                        return v.tensor_copy(R4[0:64, col:col + 128],
                                             SETP[0][0:64, 0:128])
                    return v.tensor_copy(R4[64:128, col:col + 128],
                                         SETP[1][64:128, 0:128])
                pvev[(si, h)] = DV.now()

            for si in range(NT):
                if si > 0:
                    emit_T(si - 1, 0)
                    emit_evac(si - 1, 0)
                    emit_T(si - 1, 1)
                    emit_evac(si - 1, 1)
                    emit_pv(si - 1, 0)
                    emit_pvev(si - 1, 0)
                    emit_pv(si - 1, 1)
                    emit_pvev(si - 1, 1)
                emit_sc(si, 0)
                emit_rm(si, 0)
                emit_exp(si, 0)
                emit_sc(si, 1)
                emit_rm(si, 1)
                emit_exp(si, 1)
                emit_diag(si, 0)
                emit_diag(si, 1)
            emit_T(NT - 1, 0)
            emit_evac(NT - 1, 0)
            emit_T(NT - 1, 1)
            emit_evac(NT - 1, 1)
            emit_pv(NT - 1, 0)
            emit_pvev(NT - 1, 0)
            emit_pv(NT - 1, 1)
            emit_pvev(NT - 1, 1)

            pair_free = [[evac[(NT - 1, 0)], pvev[(NT - 1, 0)]],
                         [evac[(NT - 1, 1)], pvev[(NT - 1, 1)]]]
            attn_evs.append(pvev[(NT - 1, 1)])
        attn_done = attn_evs[-1]
        dbg_dump("attnT", [(R4[:, 0:DC * S], 0, DC * S)], attn_done)

        XNb = XN
        ow_w = []
        for vc in range(DC):
            ow_w.append(gdma(XNb[:, vc * 1024:(vc + 1) * 1024],
                             dram["outwT"][vc * 128:(vc + 1) * 128, :],
                             wait=[attn_done]))
        B = R3

        x1_parts = []
        for sc in range(SC4):
            s0 = sc * 512
            @P.step("vector", wait=[attn_done, PE.now(), AC.now()], inc=DV)
            def _(v, s0=s0):
                r, rb = ROWA, ROWB
                v.tensor_copy(r[0:1, rS:rS + 512], r[32:33, s0:s0 + 512])
                v.tensor_copy(r[0:1, rQ:rQ + 512], r[96:97, s0:s0 + 512])
                v.tensor_copy(rb[0:1, cRSh:cRSh + 512], r[0:1, rS:rS + 512])
                v.tensor_sub(rb[0:1, cRSl:cRSl + 512], r[0:1, rS:rS + 512],
                             rb[0:1, cRSh:cRSh + 512])
                v.tensor_copy(rb[0:1, cMUh:cMUh + 512], r[0:1, rQ:rQ + 512])
                return v.tensor_sub(rb[0:1, cMUl:cMUl + 512], r[0:1, rQ:rQ + 512],
                                    rb[0:1, cMUh:cMUh + 512])
            rows_ev = DV.now()

            @P.step("tensor", wait=[rows_ev, DV.now()], inc=PE)
            def _(t, s0=s0):
                rb = ROWB
                ones = rb[0:1, ONES0:ONES0 + 128]
                t.matmul(PS[6][:, :], ones, rb[0:1, cRSh:cRSh + 512],
                         start=True, stop=False)
                t.matmul(PS[6][:, :], ones, rb[0:1, cRSl:cRSl + 512],
                         start=False, stop=True)
                t.matmul(PS[7][:, :], ones, rb[0:1, cMUh:cMUh + 512],
                         start=True, stop=False)
                return t.matmul(PS[7][:, :], ones, rb[0:1, cMUl:cMUl + 512],
                                start=False, stop=True)
            bcx = PE.now()

            for jc in range(DC):
                pb = PS[jc & 1]

                @P.step("tensor", wait=[ow_w[-1], attn_done, DV.now()], inc=PE)
                def _(t, pb=pb, jc=jc, s0=s0):
                    mm = None
                    for vc in range(DC):
                        mm = t.matmul(pb[:, :],
                                      XNb[:, vc * 1024 + jc * 128:
                                          vc * 1024 + (jc + 1) * 128],
                                      R4[:, vc * 2048 + s0:vc * 2048 + s0 + 512],
                                      start=(vc == 0), stop=(vc == DC - 1))
                    return mm
                mmo = PE.now()

                @P.step("vector", wait=[mmo, bcx], inc=DV)
                def _(v, pb=pb, jc=jc, s0=s0):
                    sl = slice(jc * S + s0, jc * S + s0 + 512)
                    v.tensor_tensor(B[:, sl], A[:, sl], PS[6][:, :], ALU.mult)
                    v.tensor_sub(B[:, sl], B[:, sl], PS[7][:, :])
                    return v.scalar_tensor_tensor(B[:, sl], pb[:, :],
                                                  OUTBC[:, jc:jc + 1], B[:, sl],
                                                  ALU.add, ALU.add)
            x1_parts.append(DV.now())
        x1_done = x1_parts[-1]
        dbg_dump("x1", [(B[:, 0:DC * S], 0, DC * S)], x1_done)
        fin_ready = x1_done


    if "M" in phases and "F" in phases:
        xr_done = emit_norm(R3, XN, x1_done, rms=True, w_col=N2WC,
                            ring0=0, ring_t=R4, extra_wait=[x1_done])
        dbg_dump("xr", [(XN[:, 0:DC * S], 0, DC * S)], xr_done)

        MF32 = 0
        MESEL = 2048
        MCH = 3072
        MCL = 5120
        MLNAT = 15360
        MCNAT = 15616
        MROWS = 15872

        esel_w = gdma(R4[0:8, MESEL:MESEL + 1024], dram["esel"], wait=[xr_done])
        gtw = [ball_w]

        lnat = R4[:, MLNAT:MLNAT + 256].bitcast(F32)
        cnat = R4[:, MCNAT:MCNAT + 256].bitcast(F32)
        mrows = R4[:, MROWS:MROWS + 256].bitcast(F32)
        glog_parts = []
        for sc in range(SC4):
            s0 = sc * 512

            @P.step("tensor", wait=[gtw[-1], xr_done, AC.now(), DV.now()], inc=PE)
            def _(t, s0=s0):
                mm = None
                for c in range(DC):
                    mm = t.matmul(PS[0][0:8, :], GTS[:, c * 8:(c + 1) * 8],
                                  XN[:, c * S + s0:c * S + s0 + 512],
                                  start=(c == 0), stop=(c == DC - 1))
                return mm
            lg_done = PE.now()

            @P.step("scalar", wait=[lg_done], inc=AC)
            def _(s, s0=s0, sc=sc):
                return s.copy(ROWA[0:8, sc * 512:sc * 512 + 512], PS[0][0:8, :])
            glog_parts.append(AC.now())
        lrow_done = glog_parts[-1]
        if "lgt" in dbg_out:
            @P.step("sync", wait=[lrow_done], inc=ST, inc_k=16)
            def _(sy):
                return sy.dma_start(dbg_out["lgt"], ROWA[:, 0:512])

        LALL = R4[:, MLNAT:MLNAT + 256].bitcast(F32)
        CALL = R4[:, MCNAT:MCNAT + 256].bitcast(F32)
        MR = R4[:, MROWS:MROWS + 256].bitcast(F32)
        EQ = R4[:, 0:256].bitcast(F32)
        ISH = R4[:, 256:512].bitcast(F32)

        @P.step("tensor", wait=[lrow_done, DV.now(), AC.now()], inc=PE)
        def _(t):
            mm = None
            for tc in range(NT):
                mm = t.transpose(PS[1][0:128, tc * 8:(tc + 1) * 8],
                                 ROWA[0:8, tc * 128:(tc + 1) * 128],
                                 CIDF[0:8, 0:8])
            return mm
        tp_done = PE.now()

        @P.step("vector", wait=[tp_done], inc=DV)
        def _(v):
            return v.tensor_copy(LALL[:, :], PS[1][0:128, 0:128])
        l_ev = DV.now()

        @P.step("vector", wait=[l_ev], inc=DV)
        def _(v):
            seg = bass.AP(R4, MLNAT // 2, [[8192, 128], [8, 16], [1, 8]],
                          ).bitcast(F32) if False else None
            lseg = bass.AP(LALL.tensor, LALL.offset,
                           [[LALL.ap[0][0], 128], [8, 16], [1, 8]])
            v.tensor_reduce(MR[:, 0:16], lseg, AX.X, ALU.max)
            v.tensor_copy(EQ[:, 0:16], LALL[:, 0:16])
            v.tensor_copy(EQ[:, 0:16], LALL[:, 0:16])
            m1bc = bass.AP(MR.tensor, MR.offset, [[MR.ap[0][0], 128], [1, 16], [0, 8]])
            v.tensor_tensor(CALL[:, :], LALL[:, :], m1bc, ALU.subtract)
            v.tensor_tensor(EQ[:, :], LALL[:, :], m1bc, ALU.is_equal)
            v.scalar_tensor_tensor(ISH[:, :], EQ[:, :], 1e30, LALL[:, :],
                                   ALU.mult, ALU.subtract)
            lseg2 = bass.AP(ISH.tensor, ISH.offset,
                            [[ISH.ap[0][0], 128], [8, 16], [1, 8]])
            v.tensor_reduce(MR[:, 16:32], lseg2, AX.X, ALU.min)
            v.tensor_copy(EQ[:, 0:16], LALL[:, 0:16])
            v.tensor_copy(EQ[:, 0:16], LALL[:, 0:16])
            v.scalar_tensor_tensor(MR[:, 32:48], MR[:, 16:32], -1.0, MR[:, 0:16],
                                   ALU.mult, ALU.subtract)
            m2bc = bass.AP(MR.tensor, MR.offset + 16,
                           [[MR.ap[0][0], 128], [1, 16], [0, 8]])
            return v.tensor_tensor(ISH[:, :], LALL[:, :], m2bc, ALU.add)
        t2_ev = DV.now()

        @P.step("scalar", wait=[t2_ev], inc=AC)
        def _(s):
            s.activation(CALL[:, :], CALL[:, :], AF.Exp, bias=0.0, scale=1.0)
            return s.activation(MR[:, 48:64], MR[:, 32:48], AF.Exp,
                                bias=0.0, scale=1.0)
        e_ev = AC.now()

        @P.step("vector", wait=[e_ev], inc=DV)
        def _(v):
            v.tensor_scalar_add(MR[:, 64:80], MR[:, 48:64], 1.0)
            v.tensor_copy(EQ[:, 0:16], LALL[:, 0:16])
            v.reciprocal(MR[:, 80:96], MR[:, 64:80])
            v.tensor_copy(EQ[:, 0:16], LALL[:, 0:16])
            v.tensor_copy(EQ[:, 0:16], LALL[:, 0:16])
            v.tensor_single_scalar(ISH[:, :], ISH[:, :], 0.0, ALU.is_ge)
            v.tensor_tensor(CALL[:, :], CALL[:, :], ISH[:, :], ALU.mult)
            recbc = bass.AP(MR.tensor, MR.offset + 80,
                            [[MR.ap[0][0], 128], [1, 16], [0, 8]])
            return v.tensor_tensor(CALL[:, :], CALL[:, :], recbc, ALU.mult)
        c_ev = DV.now()

        @P.step("tensor", wait=[c_ev], inc=PE)
        def _(t):
            mm = None
            for tc in range(NT):
                mm = t.transpose(PB1[0:8, tc * 128:(tc + 1) * 128],
                                 CALL[:, tc * 8:(tc + 1) * 8], CIDF[:, :])
            return mm
        ct_done = PE.now()

        @P.step("vector", wait=[ct_done], inc=DV)
        def _(v):
            v.tensor_copy(R4[0:8, MCH:MCH + 2048], PB1[0:8, 0:2048])
            v.tensor_copy(EQ[:, 0:16], LALL[:, 0:16])
            return v.tensor_sub(R4[0:8, MCL:MCL + 2048], PB1[0:8, 0:2048],
                                R4[0:8, MCH:MCH + 2048])
        comb_parts = [DV.now()]
        comb_done = comb_parts[-1]
        dbg_dump("comb", [(R4[:, MCH:MCH + 2048], 0, 2048)], comb_done)

        fin_parts = []
        for c in range(DC):
            for sc in range(SC4):
                @P.step("vector", wait=[x1_done, xr_done], inc=DV)
                def _(v, c=c, sc=sc):
                    sl = slice(c * S + sc * 512, c * S + sc * 512 + 512)
                    return v.tensor_add(A[:, sl], A[:, sl], B[:, sl])
        finadd_done = DV.now()
        dbg_dump("finadd", [(A[:, 0:DC * S], 0, DC * S)], finadd_done)

        HHALF = 0
        W3H = 16384
        W1S = [24576, 25600]
        W2S = [26624, 27648]
        H1S = [28672, 30720]

        def w12slab_dma(slot, w, e, fc, wait=()):
            wS = "w1S" if w == "w1T" else "w2S"
            return gdma(R3b[:, slot:slot + 1024], dram[wS][e][fc], wait=wait)

        def w3slab_dma1(e, fhalf, i, wait=()):
            fc = fhalf * 8 + i
            return gdma(R3b[:, W3H + i * 1024:W3H + (i + 1) * 1024],
                        dram["w3T"][e][fc * 128:(fc + 1) * 128, :], wait=wait)

        bcols = None

        def cmb_slot(e, sc):
            c0 = 8192 + ((e & 1) * 4 + sc) * 1024
            return R4[:, c0:c0 + 1024].bitcast(F32)[:, 0:512]

        mm1_ev, silu_ev, mm2_ev, hm_ev = {}, {}, {}, {}
        ymm_ev, yev_ev, cb_ev = {}, {}, {}
        w12_w, w3_w = {}, {}

        import os as _os2
        _de = int(_os2.environ.get("DBG_E", "0"))
        _dh = int(_os2.environ.get("DBG_FH", "0"))
        import os as _os3
        NEXP = int(_os3.environ.get("NEXP", str(E)))

        w12_w[(0, 0)] = w12slab_dma(W1S[0], "w1T", 0, 0, wait=[finadd_done])
        w12_w[(0, 1)] = w12slab_dma(W2S[0], "w2T", 0, 0, wait=[finadd_done])

        def emit_cb(e, ps4_free=None):
            for sc in range(SC4):
                s0 = sc * 512
                w = [esel_w]
                if e < 2:
                    w.append(comb_done)
                else:
                    w.append(yev_ev[(e - 2, 1, sc, DC - 1)])
                if sc > 0:
                    w.append(cb_ev[(e, sc - 1)])
                elif ps4_free is not None:
                    w.append(ps4_free)

                @P.step("tensor", wait=w, inc=PE)
                def _(t, s0=s0, e=e):
                    sel = R4[0:8, MESEL + e * 128:MESEL + (e + 1) * 128]
                    t.matmul(PS[4][:, :], sel, R4[0:8, MCH + s0:MCH + s0 + 512],
                             start=True, stop=False)
                    return t.matmul(PS[4][:, :], sel,
                                    R4[0:8, MCL + s0:MCL + s0 + 512],
                                    start=False, stop=True)
                cbmm = PE.now()

                @P.step("vector", wait=[cbmm], inc=DV)
                def _(v, e=e, sc=sc):
                    return v.tensor_copy(cmb_slot(e, sc), PS[4][:, :])
                cb_ev[(e, sc)] = DV.now()

        emit_cb(0)
        if "wops" in dbg_out:
            dbg_dump("wops", [(cmb_slot(0, 0), 0, 512)], cb_ev[(0, 0)])

        for e in range(NEXP):
            for fhalf in range(2):
                hb = e * 16 + fhalf * 8
                w3gate = ([ymm_ev[(e, 0, 3, DC - 1)]] if fhalf == 1 else
                          [ymm_ev[(e - 1, 1, 3, DC - 1)]] if e > 0 else
                          [finadd_done])

                for i in range(8):
                    fc = fhalf * 8 + i
                    gfc = hb + i
                    w1rdy = w12_w[(gfc, 0)]
                    w2rdy = w12_w[(gfc, 1)]
                    if gfc + 1 < NEXP * 16:
                        nf = gfc + 1
                        g1 = ([mm1_ev[(gfc - 1, 3)]] if gfc >= 1
                              else [finadd_done])
                        g2 = ([mm2_ev[(gfc - 1, 3)]] if gfc >= 1
                              else [finadd_done])
                        w12_w[(nf, 0)] = w12slab_dma(
                            W1S[nf & 1], "w1T", nf // 16, nf % 16, wait=g1)
                        w12_w[(nf, 1)] = w12slab_dma(
                            W2S[nf & 1], "w2T", nf // 16, nf % 16, wait=g2)
                    w3_w[(e, fhalf)] = w3slab_dma1(e, fhalf, i, wait=w3gate)

                    h1p = H1S[fc & 1]
                    for sc in range(SC4):
                        s0 = sc * 512
                        pb = PS[sc]
                        w = [w1rdy]
                        if gfc >= 1:
                            w.append(silu_ev[(gfc - 1, sc)])
                        else:
                            w += [xr_done, lrow_done, l_ev]

                        @P.step("tensor", wait=w, inc=PE)
                        def _(t, pb=pb, s0=s0, w1s=W1S[gfc & 1]):
                            mm = None
                            for c in range(DC):
                                mm = t.matmul(pb[:, :],
                                              R3b[:, w1s + c * 128:w1s + (c + 1) * 128],
                                              XN[:, c * S + s0:c * S + s0 + 512],
                                              start=(c == 0), stop=(c == DC - 1))
                            return mm
                        mm1_ev[(gfc, sc)] = PE.now()

                        ws = [mm1_ev[(gfc, sc)]]
                        if gfc >= 2:
                            ws.append(hm_ev[(gfc - 2, sc)])
                        else:
                            ws.append(finadd_done)

                        @P.step("scalar", wait=ws, inc=AC)
                        def _(s, pb=pb, s0=s0, h1p=h1p, fc=fc, e=e):
                            return s.activation(R3b[:, h1p + s0:h1p + s0 + 512],
                                                pb[:, :], AF.Silu,
                                                bias=BAL1[:, e * 16 + fc:e * 16 + fc + 1],
                                                scale=1.0)
                        silu_ev[(gfc, sc)] = AC.now()

                    for sc in range(SC4):
                        s0 = sc * 512
                        pb = PS[4 + (sc & 1)]
                        w = [w2rdy]
                        if sc >= 2:
                            w.append(hm_ev[(gfc, sc - 2)])
                        elif gfc >= 1:
                            w.append(hm_ev[(gfc - 1, 2 + sc)])
                        else:
                            w.append(comb_done)
                        if (i == 0 and sc == 0 and fhalf == 1
                                and (e + 1, 3) in cb_ev):
                            w.append(cb_ev[(e + 1, 3)])

                        @P.step("tensor", wait=w, inc=PE)
                        def _(t, pb=pb, s0=s0, w2s=W2S[gfc & 1]):
                            mm = None
                            for c in range(DC):
                                mm = t.matmul(pb[:, :],
                                              R3b[:, w2s + c * 128:w2s + (c + 1) * 128],
                                              XN[:, c * S + s0:c * S + s0 + 512],
                                              start=(c == 0), stop=(c == DC - 1))
                            return mm
                        mm2_ev[(gfc, sc)] = PE.now()

                        wh = [mm2_ev[(gfc, sc)], silu_ev[(gfc, sc)]]
                        if i == 0:
                            if fhalf == 1:
                                wh.append(ymm_ev[(e, 0, sc, DC - 1)])
                            elif e > 0:
                                wh.append(ymm_ev[(e - 1, 1, sc, DC - 1)])

                        @P.step("vector", wait=wh, inc=DV)
                        def _(v, pb=pb, s0=s0, h1p=h1p, i=i, fc=fc, e=e):
                            hcol = HHALF + i * 2048
                            return v.scalar_tensor_tensor(
                                R3b[:, hcol + s0:hcol + s0 + 512], pb[:, :],
                                BAL2[:, e * 16 + fc:e * 16 + fc + 1],
                                R3b[:, h1p + s0:h1p + s0 + 512],
                                ALU.add, ALU.mult)
                        hm_ev[(gfc, sc)] = DV.now()

                if e == _de and fhalf == _dh:
                    dbg_dump("h0", [(R3b[:, 0:8 * S], 0, 8 * S)],
                             hm_ev[(hb + 7, 3)])

                for sc in range(SC4):
                    s0 = sc * 512
                    for dcO in range(DC):
                        pb = PS[6 + (dcO & 1)]
                        w = [w3_w[(e, fhalf)], hm_ev[(hb + 7, sc)]]
                        if dcO >= 2:
                            w.append(yev_ev[(e, fhalf, sc, dcO - 2)])
                        elif sc >= 1:
                            w.append(yev_ev[(e, fhalf, sc - 1, 6 + dcO)])
                        elif fhalf == 1:
                            w.append(yev_ev[(e, 0, 3, 6 + dcO)])
                        elif e > 0:
                            w.append(yev_ev[(e - 1, 1, 3, 6 + dcO)])
                        else:
                            w.append(comb_done)

                        @P.step("tensor", wait=w, inc=PE)
                        def _(t, pb=pb, s0=s0, dcO=dcO):
                            mm = None
                            for i2 in range(8):
                                mm = t.matmul(pb[:, :],
                                              R3b[:, W3H + i2 * 1024 + dcO * 128:
                                                  W3H + i2 * 1024 + (dcO + 1) * 128],
                                              R3b[:, HHALF + i2 * 2048 + s0:
                                                  HHALF + i2 * 2048 + s0 + 512],
                                              start=(i2 == 0), stop=(i2 == 7))
                            return mm
                        ymm_ev[(e, fhalf, sc, dcO)] = PE.now()
                        mmy = ymm_ev[(e, fhalf, sc, dcO)]
                        if (e == _de and fhalf == _dh and sc == 0 and dcO == 0
                                and "gn0" in dbg_out):
                            @P.step("scalar", wait=[mmy], inc=AC)
                            def _(s, pb=pb):
                                return s.copy(R4[:, 14336:15360].bitcast(F32),
                                              pb[:, :])
                            dbg_dump("gn0",
                                     [(R4[:, 14336:15360].bitcast(F32), 0, 512)],
                                     AC.now())

                        @P.step("vector", wait=[mmy], inc=DV)
                        def _(v, pb=pb, s0=s0, dcO=dcO, fhalf=fhalf, e=e, sc=sc):
                            sl = slice(dcO * S + s0, dcO * S + s0 + 512)
                            tmp = R4[:, MF32 + 1024:MF32 + 2048].bitcast(F32)
                            comb = cmb_slot(e, sc)
                            if fhalf == 1:
                                v.scalar_tensor_tensor(tmp[:, 0:512], pb[:, :],
                                                       BAL3[:, e * 8 + dcO:e * 8 + dcO + 1],
                                                       comb, ALU.add, ALU.mult)
                            else:
                                v.tensor_tensor(tmp[:, 0:512], pb[:, :], comb,
                                                ALU.mult)
                            return v.tensor_add(A[:, sl], A[:, sl], tmp[:, 0:512])
                        yev_ev[(e, fhalf, sc, dcO)] = DV.now()

                    if fhalf == 0 and sc == 1 and e + 1 < NEXP:
                        emit_cb(e + 1, ps4_free=hm_ev[(e * 16 + 7, 2)])
        moe_done = DV.now()
        dbg_dump("fin", [(A[:, 0:DC * S], 0, DC * S)], moe_done)
        dbg_dump("gn1", [(R4[:, MCH:MCH + 2048], 0, 2048)], moe_done)
        fin_ready = moe_done

    for c in range(DC):
        @P.step("sync", wait=[fin_ready, DV.now(), AC.now(), PE.now()],
                inc=ST, inc_k=16)
        def _(sy, c=c):
            return sy.dma_start(out[c * 128:(c + 1) * 128, :], A[:, c * S:(c + 1) * S])

    @P.step("sync", wait=[(ST, ST.n)])
    def _(sy):
        return None

    P.build()
    return nc, dram, dbg_out, sbuf_left



def kernel(**inputs):
    X = np.ascontiguousarray(np.asarray(inputs["L0_attn_X"]), dtype=np.float32)
    shared = _prep_shared(inputs)
    nc, dram, _, _ = build_nc()
    in_maps = []
    for b in range(8):
        m = dict(shared)
        m["xT"] = np.ascontiguousarray(X[b].T)
        in_maps.append(m)
    res = run_bass_kernel_spmd(nc, in_maps, core_ids=list(range(8)))
    return np.stack([np.asarray(r["out"]).T for r in res.results], axis=0)
